# version 02 (final)
# baseline (speedup 1.0000x reference)
import numpy as np
from contextlib import ExitStack

import concourse.bass as bass
import concourse.mybir as mybir
from concourse.bass_utils import run_bass_kernel_spmd

F32 = mybir.dt.float32
BF16 = mybir.dt.bfloat16
AF = mybir.ActivationFunctionType
ALU = mybir.AluOpType
AX = mybir.AxisListType

W = 256
TOUT = 2048
T = W + TOUT
D = 1024
NST = 16
NTB = TOUT // 128
CH_T = [(0, 512), (512, 512), (1024, 512), (1536, 512), (2048, 256)]
CH_CM = [(256, 512), (768, 512), (1280, 512), (1792, 512)]
PEEL = [128.0, 64.0, 32.0, 16.0, 8.0, 4.0, 2.0, 1.0]

STREAMS = ("sync", "gp", "pe", "act", "dve")
SEMK = {"sync": "SL", "gp": "SG", "pe": "SP", "act": "SA", "dve": "SV"}


def build_nc():
    nc = bass.Bass("TRN2", target_bir_lowering=False)

    d_xT = nc.dram_tensor("xpadT", [D, T], F32, kind="ExternalInput").ap()
    d_xo = nc.dram_tensor("xown", [TOUT, D], F32, kind="ExternalInput").ap()
    d_wxp = nc.dram_tensor("wxp", [D, 128], F32, kind="ExternalInput").ap()
    d_wdt = nc.dram_tensor("wdt", [64, D], F32, kind="ExternalInput").ap()
    d_bdt = nc.dram_tensor("bdt", [128, 8], F32, kind="ExternalInput").ap()
    d_alt = nc.dram_tensor("alt", [128, 128], F32, kind="ExternalInput").ap()
    d_rhsT = nc.dram_tensor("rhsT", [D, D], F32, kind="ExternalInput").ap()
    d_ysc = nc.dram_tensor("ysc", [4, 128, TOUT], F32, kind="Internal").ap()
    d_out = nc.dram_tensor("out", [TOUT, D], F32, kind="ExternalOutput").ap()

    es = ExitStack()
    SB = lambda n, sh, dt=F32: es.enter_context(nc.sbuf_tensor(n, sh, dt))
    PS = lambda n, sh, dt=F32: es.enter_context(nc.psum_tensor(n, sh, dt))

    with es:
        A = [SB(f"bga{i}", [128, T]) for i in range(4)]
        B = [SB(f"bgb{i}", [128, T]) for i in range(4)]
        C = [SB(f"bgc{i}", [128, T]) for i in range(4)]
        Y = [SB(f"y{i}", [128, TOUT]) for i in range(4)]
        repB = SB("repB", [128, T])
        repC = SB("repC", [128, T])
        pb = [SB(f"pb{i}", [128, T]) for i in range(2)]
        xp = SB("xp", [128, T])
        wxp = SB("wxp_sb", [128, 1024])
        wdt = SB("wdt_sb", [128, 1024])
        bdt = SB("bdt_sb", [128, 8])
        AT = SB("AT", [128, 128])
        sel = SB("sel", [48, 16 * 128])
        ident = SB("ident", [128, 128])
        cols = SB("cols", [128, 40])
        srow8 = SB("srow8", [8, 128])
        srw = SB("srw", [128, 16])
        vcol = SB("vcol", [128, 1])
        lncol = SB("lncol", [128, 1])
        rstd_t = SB("rstd_t", [128, 1])
        nmr_t = SB("nmr_t", [128, 1])
        qz = SB("qz", [128, 2048], BF16)

        qrhs = [wxp[:, 0:512].bitcast(BF16), wxp[:, 512:1024].bitcast(BF16),
                wdt[:, 0:512].bitcast(BF16), wdt[:, 512:1024].bitcast(BF16),
                pb[0][:, 0:512].bitcast(BF16), pb[0][:, 512:1024].bitcast(BF16),
                pb[0][:, 1024:1536].bitcast(BF16), pb[0][:, 1536:2048].bitcast(BF16)]

        pA = PS("pA", [128, 1024])
        pB = PS("pB", [128, 1024])
        pC = [PS("pC0", [128, 512]), PS("pC1", [128, 512])]

        sems = {k: es.enter_context(nc.semaphore(k))
                for k in ("SL", "SG", "SP", "SA", "SV", "SY")}

        prog = {s: [] for s in STREAMS}
        cnt = {"SL": 0, "SG": 0, "SP": 0, "SA": 0, "SV": 0, "SY": 0}

        def emit(stream, fn, waits=(), dma=False, hard=()):
            key = "SY" if (stream == "gp" and dma) else ("SL" if stream == "sync" else SEMK[stream])
            inc = 16 if dma else 1
            own = SEMK[stream]
            wl = []
            for tok in waits:
                if tok is None:
                    continue
                k, v = tok
                if v <= 0 or k == own:
                    continue
                wl.append((k, v))
            for tok in hard:
                if tok is not None and tok[1] > 0:
                    wl.append(tok)
            prog[stream].append((wl, fn, key, inc))
            cnt[key] += inc
            return (key, cnt[key])

        def barrier(stream, waits):
            prog[stream].append(([t for t in waits if t is not None], None, None, 0))

        emit("gp", lambda: nc.gpsimd.memset(ident[:], 0.0))
        tok_ident = emit("gp", lambda: nc.gpsimd.affine_select(
            out=ident[:], in_=ident[:], compare_op=ALU.not_equal, fill=1.0,
            base=0, pattern=[[-1, 128]], channel_multiplier=1))
        emit("gp", lambda: nc.gpsimd.memset(sel[:], 0.0))
        emit("gp", lambda: nc.gpsimd.affine_select(
            out=sel[0:16, :].rearrange("p (n f) -> p n f", n=16),
            in_=sel[0:16, :].rearrange("p (n f) -> p n f", n=16),
            compare_op=ALU.not_equal, fill=1.0, base=0,
            pattern=[[-1, 16], [0, 128]], channel_multiplier=1))
        tok_sel = emit("gp", lambda: nc.gpsimd.affine_select(
            out=sel[32:48, :].rearrange("p (n f) -> p n f", n=16),
            in_=sel[32:48, :].rearrange("p (n f) -> p n f", n=16),
            compare_op=ALU.not_equal, fill=1.0, base=0,
            pattern=[[-1, 16], [0, 128]], channel_multiplier=1))

        wxp_toks = [emit("sync", lambda ki=ki: nc.sync.dma_start(
            out=wxp[:, ki * 128:(ki + 1) * 128], in_=d_wxp[ki * 128:(ki + 1) * 128, :]),
            dma=True) for ki in range(8)]
        tok_wxp = wxp_toks[-1]
        tok_wdt = emit("sync", lambda: nc.sync.dma_start(out=wdt[64:128, :], in_=d_wdt), dma=True)
        tok_bdt = emit("sync", lambda: nc.sync.dma_start(out=bdt[:], in_=d_bdt), dma=True)
        tok_alt = emit("sync", lambda: nc.sync.dma_start(out=AT[:], in_=d_alt), dma=True)

        def load_xT(half, war):
            toks = []
            for jj in range(4):
                j = half * 4 + jj
                toks.append(emit("sync", lambda j=j, jj=jj: nc.sync.dma_start(
                    out=A[jj][:], in_=d_xT[j * 128:(j + 1) * 128, :]), dma=True, waits=war))
            return toks

        a1 = emit("act", lambda: nc.scalar.activation(pb[0][:, 0:128], AT[:], AF.Exp),
                  waits=[tok_alt])
        tok_AT = emit("dve", lambda: nc.vector.tensor_single_scalar(AT[:], pb[0][:, 0:128], -1.0, ALU.mult),
                      waits=[a1])

        pC_reader = [None, None]
        xTA = load_xT(0, [])
        xp_evacA = []
        for ci, (off, ln) in enumerate(CH_T):
            par = ci % 2
            mm = None
            for kk in range(4):
                mm = emit("pe", lambda kk=kk, off=off, ln=ln, par=par: nc.tensor.matmul(
                    pC[par][:, 0:ln], wxp[:, kk * 128:(kk + 1) * 128], A[kk][:, off:off + ln],
                    start=(kk == 0), stop=(kk == 3)),
                    waits=[xTA[kk], tok_wxp, pC_reader[par]])
            ev = emit("act", lambda off=off, ln=ln, par=par: nc.scalar.copy(
                xp[:, off:off + ln], pC[par][:, 0:ln]), waits=[mm])
            pC_reader[par] = ev
            xp_evacA.append(ev)
        xp_passA_end = ("SP", cnt["SP"])
        xTB = load_xT(1, [xp_passA_end])
        tok_xp = None
        for ci, (off, ln) in enumerate(CH_T):
            par = ci % 2
            mm = None
            for kk in range(4):
                mm = emit("pe", lambda kk=kk, off=off, ln=ln, par=par: nc.tensor.matmul(
                    pC[par][:, 0:ln], wxp[:, (4 + kk) * 128:(5 + kk) * 128], A[kk][:, off:off + ln],
                    start=(kk == 0), stop=(kk == 3)),
                    waits=[xTB[kk], pC_reader[par]])
            tok_xp = emit("dve", lambda off=off, ln=ln, par=par: nc.vector.tensor_add(
                xp[:, off:off + ln], xp[:, off:off + ln], pC[par][:, 0:ln]),
                waits=[mm, xp_evacA[ci]])
            pC_reader[par] = tok_xp
        xp_passB_end = ("SP", cnt["SP"])

        abar = [A[0], A[1]]
        hbuf = A[2]

        st = {
            "abar_r": [None, None],
            "pb_gp": [None, None],
            "repB_r": None, "repC_r": None,
            "pA_r": None,
            "y_dve": [None] * 4,
            "ysc": [None] * 4,
        }

        def do_half(half, xT_toks):
            dt_done = []
            for jj in range(4):
                j = half * 4 + jj
                last = None
                for ci, (off, ln) in enumerate(CH_T):
                    par = ci % 2
                    mm = emit("pe", lambda j=j, off=off, ln=ln, par=par: nc.tensor.matmul(
                        pC[par][:, 0:ln], wdt[64:128, j * 128:(j + 1) * 128],
                        xp[64:128, off:off + ln], start=True, stop=True),
                        waits=[tok_xp, tok_wdt, pC_reader[par]])
                    rl = emit("act", lambda jj=jj, j=j, off=off, ln=ln, par=par:
                              nc.scalar.activation(C[jj][:, off:off + ln], pC[par][:, 0:ln],
                                                   AF.Relu, bias=bdt[:, j:j + 1]),
                              waits=[mm, tok_bdt])
                    ab = emit("act", lambda jj=jj, j=j, off=off, ln=ln, par=par:
                              nc.scalar.activation(B[jj][:, off:off + ln], pC[par][:, 0:ln],
                                                   AF.Abs, bias=bdt[:, j:j + 1]),
                              waits=[mm])
                    pC_reader[par] = ab
                    emit("act", lambda jj=jj, off=off, ln=ln: nc.scalar.activation(
                        B[jj][:, off:off + ln], B[jj][:, off:off + ln], AF.Exp, scale=-1.0))
                    ll = emit("act", lambda jj=jj, off=off, ln=ln: nc.scalar.activation(
                        B[jj][:, off:off + ln], B[jj][:, off:off + ln], AF.Ln, bias=1.0))
                    last = emit("dve", lambda jj=jj, off=off, ln=ln: nc.vector.tensor_add(
                        B[jj][:, off:off + ln], B[jj][:, off:off + ln], C[jj][:, off:off + ln]),
                        waits=[ll, rl])
                dt_done.append(last)
            u_done = []
            for jj in range(4):
                u_done.append(emit("dve", lambda jj=jj: nc.vector.tensor_mul(
                    C[jj][:], B[jj][:], A[jj][:]), waits=[dt_done[jj], xT_toks[jj]]))
            st["abar_r"] = [u_done[0], u_done[1]]
            h_guard = u_done[2]

            for n in range(NST):
                idx = half * NST + n
                bm_ev = None
                for g, (off, ln) in enumerate(CH_T):
                    po = (g % 2) * 512
                    mm = emit("pe", lambda n=n, off=off, ln=ln, po=po: nc.tensor.matmul(
                        pA[:, po:po + ln], sel[0:16, n * 128:(n + 1) * 128],
                        xp[0:16, off:off + ln], start=True, stop=True),
                        waits=[tok_sel, tok_xp, st["pA_r"]])
                    bm_ev = emit("act", lambda off=off, ln=ln, po=po: nc.scalar.copy(
                        repB[:, off:off + ln], pA[:, po:po + ln]),
                        waits=[mm, st["repB_r"]])
                    st["pA_r"] = bm_ev
                cm_ev = None
                for g, (off, ln) in enumerate(CH_CM):
                    po = (g % 2) * 512
                    mm = emit("pe", lambda n=n, off=off, ln=ln, po=po: nc.tensor.matmul(
                        pA[:, po:po + ln], sel[32:48, n * 128:(n + 1) * 128],
                        xp[32:48, off:off + ln], start=True, stop=True),
                        waits=[st["pA_r"]])
                    cm_ev = emit("act", lambda off=off, ln=ln, po=po: nc.scalar.copy(
                        repC[:, off:off + ln], pA[:, po:po + ln]),
                        waits=[mm, st["repC_r"]])
                    st["pA_r"] = cm_ev
                for jj in range(4):
                    j = half * 4 + jj
                    it = idx * 4 + jj
                    apar = it % 2
                    aw = emit("act", lambda jj=jj, j=j, n=n, apar=apar: nc.scalar.activation(
                        abar[apar][:], B[jj][:], AF.Exp,
                        scale=AT[:, j * 16 + n:j * 16 + n + 1]),
                        waits=[dt_done[jj], tok_AT, st["abar_r"][apar]])
                    bu = emit("dve", lambda jj=jj, apar=apar: nc.vector.tensor_mul(
                        pb[apar][:], C[jj][:], repB[:]),
                        waits=[bm_ev, st["pb_gp"][apar]])
                    st["repB_r"] = bu
                    sc = emit("dve", lambda apar=apar: nc.vector.tensor_tensor_scan(
                        hbuf[:], abar[apar][:], pb[apar][:], 0.0, ALU.mult, ALU.add),
                        waits=[aw, h_guard])
                    h_guard = None
                    st["abar_r"][apar] = sc
                    if n == 0:
                        pm = emit("dve", lambda jj=jj: nc.vector.tensor_mul(
                            Y[jj][:], hbuf[:, W:], repC[:, W:]),
                            waits=[cm_ev, st["ysc"][jj]])
                        st["y_dve"][jj] = pm
                    else:
                        pm = emit("dve", lambda apar=apar: nc.vector.tensor_mul(
                            pb[apar][:, 0:TOUT], hbuf[:, W:], repC[:, W:]),
                            waits=[cm_ev])
                        ya = emit("gp", lambda jj=jj, apar=apar: nc.gpsimd.tensor_add(
                            Y[jj][:], Y[jj][:], pb[apar][:, 0:TOUT]),
                            waits=[pm, st["y_dve"][jj]])
                        st["pb_gp"][apar] = ya
                        st["y_dve"][jj] = ya
                    st["repC_r"] = pm
            if half == 0:
                for jj in range(4):
                    st["ysc"][jj] = emit("gp", lambda jj=jj: nc.gpsimd.dma_start(
                        out=d_ysc[jj], in_=Y[jj][:]), dma=True)

        do_half(0, xTA := load_xT(0, [xp_passB_end]))
        half_a_end = [("SV", cnt["SV"]), ("SA", cnt["SA"]), ("SG", cnt["SG"])]
        do_half(1, load_xT(1, half_a_end))

        VEND = ("SV", cnt["SV"])
        AEND = ("SA", cnt["SA"])
        GEND = ("SG", cnt["SG"])
        PEND = ("SP", cnt["SP"])
        MEND = [VEND, AEND, GEND, PEND]

        rhsT_toks = [emit("sync", lambda ct=ct: nc.sync.dma_start(
            out=B[ct // 2][:, (ct % 2) * 1024:(ct % 2) * 1024 + 1024],
            in_=d_rhsT[ct * 128:(ct + 1) * 128, :]), dma=True, waits=MEND)
            for ct in range(8)]

        r1 = A[0][:, 0:1024]; scr = A[0][:, 1024:2048]
        qv = A[1][:, 0:1024]; fr = A[1][:, 1024:2048]
        t1q = A[2][:, 0:1024]; qf = A[2][:, 1024:2048]
        r2 = A[3][:, 0:1024]; scr2 = A[3][:, 1024:2048]
        zT = [C[0][:, 0:1024], C[0][:, 1024:2048]]
        xo = [C[1][:, 0:1024], C[1][:, 1024:2048]]
        outst = [C[2][:, 0:1024], C[2][:, 1024:2048]]
        srep = C[3][:, 0:1024]
        y0tb = [repB[:, 0:512], repB[:, 512:1024]]

        def peel_round(src_ap, dst_ap, extra_waits):
            emit("dve", lambda: nc.vector.tensor_single_scalar(fr, src_ap, 128.5, ALU.add),
                 waits=extra_waits)
            for k in PEEL:
                emit("dve", lambda k=k: nc.vector.tensor_scalar(
                    t1q, fr, k, -k, ALU.is_ge, ALU.mult))
                emit("dve", lambda: nc.vector.tensor_add(fr, fr, t1q))
            emit("dve", lambda: nc.vector.scalar_tensor_tensor(
                dst_ap, fr, -1.0, src_ap, ALU.mult, ALU.add))
            return emit("dve", lambda: nc.vector.tensor_single_scalar(dst_ap, dst_ap, 0.5, ALU.add))

        pB_reader = None
        for ct in range(8):
            rt = B[ct // 2][:, (ct % 2) * 1024:(ct % 2) * 1024 + 1024]
            t = emit("dve", lambda rt=rt: nc.vector.reduce_max(
                cols[:, 15:16], rt, AX.X, apply_absolute_value=True),
                waits=[rhsT_toks[ct]] + MEND)
            t = emit("dve", lambda: nc.vector.tensor_single_scalar(cols[:, 16:17], cols[:, 15:16], 1e-12, ALU.max),
                     hard=[t])
            t = emit("dve", lambda ct=ct: nc.vector.tensor_single_scalar(
                cols[:, 26 + ct:27 + ct], cols[:, 16:17], 1.0 / 127.0, ALU.mult), hard=[t])
            t = emit("dve", lambda: nc.vector.reciprocal(cols[:, 17:18], cols[:, 16:17]), hard=[t])
            rq = emit("dve", lambda: nc.vector.tensor_single_scalar(cols[:, 18:19], cols[:, 17:18], 127.0, ALU.mult),
                      hard=[t])
            vv = emit("act", lambda rt=rt: nc.scalar.activation(
                qv, rt, AF.Copy, scale=cols[:, 18:19]), waits=[rq] + MEND)
            qd = peel_round(qv, qf, [vv])
            for ki in range(8):
                tp = emit("pe", lambda ki=ki: nc.tensor.transpose(
                    pB[:, ki * 128:(ki + 1) * 128], qf[:, ki * 128:(ki + 1) * 128], ident[:]),
                    waits=[qd, tok_ident, pB_reader] + MEND)
                pB_reader = emit("act", lambda ki=ki, ct=ct: nc.scalar.copy(
                    qrhs[ki][:, ct * 128:(ct + 1) * 128], pB[:, ki * 128:(ki + 1) * 128]),
                    waits=[tp])
        tok_qrhs = pB_reader

        tps = emit("pe", lambda: nc.tensor.transpose(pA[0:8, 0:128], cols[:, 26:34], ident[:]),
                   waits=[tok_qrhs, st["pA_r"]] + MEND)
        evs = emit("act", lambda: nc.scalar.copy(srow8[:], pA[0:8, 0:128]), waits=[tps])
        last_rep = None
        for ct in range(8):
            last_rep = emit("pe", lambda ct=ct: nc.tensor.matmul(
                pA[:, ct * 128:(ct + 1) * 128], sel[0:8, ct * 128:(ct + 1) * 128], srow8[:],
                start=True, stop=True), waits=[evs])
        tok_srep = emit("act", lambda: nc.scalar.copy(srep, pA[:]), waits=[last_rep, VEND])
        pA_tail_r = tok_srep

        xo_r = [None, None]
        y0_r = [None, None]
        zT_r = [None, None]
        qz_r = [None, None]
        outst_r = [None, None]
        pCff_r = [pC_reader[0], pC_reader[1]]
        r1_war = None
        r2_war = None
        qv_war = None

        def ln_stats(src_ap, c_m, waits):
            s2 = emit("act", lambda: nc.scalar.activation(scr2, src_ap, AF.Square), waits=waits)
            t = emit("dve", lambda: nc.vector.reduce_sum(
                cols[:, c_m:c_m + 1], src_ap, AX.X), waits=waits)
            t = emit("dve", lambda: nc.vector.reduce_sum(
                cols[:, c_m + 1:c_m + 2], scr2, AX.X), waits=[s2], hard=[t])
            t = emit("dve", lambda: nc.vector.tensor_single_scalar(
                cols[:, c_m + 2:c_m + 3], cols[:, c_m:c_m + 1], 1.0 / 1024.0, ALU.mult),
                hard=[t])
            t = emit("dve", lambda: nc.vector.tensor_single_scalar(
                cols[:, c_m + 3:c_m + 4], cols[:, c_m + 1:c_m + 2], 1.0 / 1024.0, ALU.mult),
                hard=[t])
            t = emit("dve", lambda: nc.vector.tensor_mul(
                cols[:, c_m + 4:c_m + 5], cols[:, c_m + 2:c_m + 3], cols[:, c_m + 2:c_m + 3]),
                hard=[t])
            t = emit("dve", lambda: nc.vector.tensor_sub(
                vcol[:], cols[:, c_m + 3:c_m + 4], cols[:, c_m + 4:c_m + 5]), hard=[t])
            vv = emit("dve", lambda: nc.vector.tensor_single_scalar(
                vcol[:], vcol[:], 1e-5, ALU.add), hard=[t])
            lnt = emit("act", lambda: nc.scalar.activation(lncol[:], vcol[:], AF.Ln), waits=[vv])
            lh = emit("dve", lambda: nc.vector.tensor_single_scalar(lncol[:], lncol[:], -0.5, ALU.mult),
                      waits=[lnt])
            l2 = emit("act", lambda: nc.scalar.activation(rstd_t[:], lncol[:], AF.Exp), waits=[lh])
            t = emit("dve", lambda: nc.vector.tensor_mul(
                nmr_t[:], cols[:, c_m + 2:c_m + 3], rstd_t[:]), waits=[l2])
            return emit("dve", lambda: nc.vector.tensor_single_scalar(nmr_t[:], nmr_t[:], -1.0, ALU.mult),
                        hard=[t])

        for tb in range(NTB):
            par = tb % 2
            xol = emit("sync", lambda tb=tb, par=par: nc.sync.dma_start(
                out=xo[par], in_=d_xo[tb * 128:(tb + 1) * 128, :]), dma=True,
                waits=[xo_r[par]] + MEND)
            y0l = emit("sync", lambda tb=tb, par=par: nc.sync.dma_start(
                out=y0tb[par].rearrange("p (a f) -> p a f", a=4),
                in_=d_ysc[:, :, tb * 128:(tb + 1) * 128].rearrange("a p f -> p a f")),
                dma=True, waits=[y0_r[par], st["ysc"][3]] + MEND)
            tp_last = None
            for j in range(8):
                if j < 4:
                    tp_last = emit("pe", lambda j=j, par=par: nc.tensor.transpose(
                        pA[:, j * 128:(j + 1) * 128], y0tb[par][:, j * 128:(j + 1) * 128], ident[:]),
                        waits=[y0l, tok_ident, pA_tail_r])
                    if j == 3:
                        y0_r[par] = tp_last
                else:
                    tp_last = emit("pe", lambda j=j, tb=tb: nc.tensor.transpose(
                        pA[:, j * 128:(j + 1) * 128], Y[j - 4][:, tb * 128:(tb + 1) * 128], ident[:]),
                        waits=[GEND, tok_ident, pA_tail_r])
            r1t = emit("dve", lambda par=par: nc.vector.tensor_add(r1, pA[:], xo[par]),
                       waits=[tp_last, xol, r1_war])
            pA_tail_r = r1t
            xo_r[par] = r1t
            nm = ln_stats(r1, 0, [r1t])
            zt = emit("act", lambda par=par: nc.scalar.activation(
                zT[par], r1, AF.Identity, scale=rstd_t[:], bias=nmr_t[:]),
                waits=[nm, zT_r[par]])
            r1_war = zt
            t = emit("dve", lambda par=par: nc.vector.reduce_max(
                cols[:, 11:12], zT[par], AX.X, apply_absolute_value=True), waits=[zt])
            t = emit("dve", lambda: nc.vector.tensor_single_scalar(cols[:, 12:13], cols[:, 11:12], 1e-12, ALU.max),
                     hard=[t])
            t = emit("dve", lambda tb=tb: nc.vector.tensor_single_scalar(
                srw[:, tb:tb + 1], cols[:, 12:13], 1.0 / 127.0, ALU.mult), hard=[t])
            t = emit("dve", lambda: nc.vector.reciprocal(cols[:, 13:14], cols[:, 12:13]), hard=[t])
            rq = emit("dve", lambda: nc.vector.tensor_single_scalar(cols[:, 14:15], cols[:, 13:14], 127.0, ALU.mult),
                      hard=[t])
            vv = emit("act", lambda par=par: nc.scalar.activation(
                qv, zT[par], AF.Copy, scale=cols[:, 14:15]), waits=[rq, qv_war])
            qd = peel_round(qv, qf, [vv])
            tpq = None
            for ki in range(8):
                tpq = emit("pe", lambda ki=ki: nc.tensor.transpose(
                    pB[:, ki * 128:(ki + 1) * 128], qf[:, ki * 128:(ki + 1) * 128], ident[:]),
                    waits=[qd, pB_reader])
            qev = emit("act", lambda par=par: nc.scalar.copy(
                qz[:, par * 1024:(par + 1) * 1024], pB[:]), waits=[tpq, qz_r[par]])
            pB_reader = qev
            r2_done = []
            for cc in range(2):
                mm = None
                for ki in range(8):
                    mm = emit("pe", lambda ki=ki, cc=cc, par=par: nc.tensor.matmul(
                        pC[cc][:, 0:512],
                        qz[:, par * 1024 + ki * 128:par * 1024 + (ki + 1) * 128],
                        qrhs[ki][:, cc * 512:(cc + 1) * 512],
                        start=(ki == 0), stop=(ki == 7)),
                        waits=[qev, tok_qrhs, pCff_r[cc]])
                if cc == 1:
                    qz_r[par] = mm
                f1 = emit("dve", lambda cc=cc: nc.vector.tensor_mul(
                    qv[:, cc * 512:(cc + 1) * 512], pC[cc][:, 0:512],
                    srep[:, cc * 512:(cc + 1) * 512]),
                    waits=[mm, tok_srep])
                pCff_r[cc] = f1
                f2 = emit("act", lambda cc=cc, tb=tb: nc.scalar.activation(
                    fr[:, cc * 512:(cc + 1) * 512], qv[:, cc * 512:(cc + 1) * 512], AF.Copy,
                    scale=srw[:, tb:tb + 1]), waits=[f1])
                r2_done.append(emit("dve", lambda cc=cc, par=par: nc.vector.tensor_add(
                    r2[:, cc * 512:(cc + 1) * 512], fr[:, cc * 512:(cc + 1) * 512],
                    zT[par][:, cc * 512:(cc + 1) * 512]),
                    waits=[f2, r2_war]))
            zT_r[par] = r2_done[-1]
            qv_war = r2_done[-1]
            nm2 = ln_stats(r2, 16, r2_done)
            oc = emit("act", lambda par=par: nc.scalar.activation(
                outst[par], r2, AF.Identity, scale=rstd_t[:], bias=nmr_t[:]),
                waits=[nm2, outst_r[par]])
            r2_war = oc
            outst_r[par] = emit("gp", lambda tb=tb, par=par: nc.gpsimd.dma_start(
                out=d_out[tb * 128:(tb + 1) * 128, :], in_=outst[par]), dma=True,
                waits=[oc])

        barrier("gp", [("SY", cnt["SY"])])

        with nc.Block() as blk:
            def runner(stream):
                def _run(proxy):
                    for wl, fn, key, inc in prog[stream]:
                        for (k, v) in wl:
                            proxy.wait_ge(sems[k], v)
                        if fn is not None:
                            fn().then_inc(sems[key], inc)
                return _run
            blk.sync(runner("sync"))
            blk.gpsimd(runner("gp"))
            blk.tensor(runner("pe"))
            blk.scalar(runner("act"))
            blk.vector(runner("dve"))

    return nc


_nc_cache = None


def kernel(L0_attn_x, L0_attn_W_xproj, L0_attn_W_dt, L0_attn_b_dt, L0_attn_A_log, L0_ffn_rhs):
    global _nc_cache
    x = np.ascontiguousarray(np.asarray(L0_attn_x, dtype=np.float32))
    Wxp = np.asarray(L0_attn_W_xproj, dtype=np.float32)
    Wdt = np.ascontiguousarray(np.asarray(L0_attn_W_dt, dtype=np.float32))
    bdtv = np.asarray(L0_attn_b_dt, dtype=np.float32)
    Alog = np.asarray(L0_attn_A_log, dtype=np.float32)
    rhs = np.asarray(L0_ffn_rhs, dtype=np.float32)

    wxp_r = np.zeros((D, 128), np.float32)
    wxp_r[:, 0:16] = Wxp[:, 0:16]
    wxp_r[:, 32:48] = Wxp[:, 16:32]
    wxp_r[:, 64:128] = Wxp[:, 32:96]
    bdt_arr = np.ascontiguousarray(bdtv.reshape(8, 128).T)
    alt = np.zeros((128, 128), np.float32)
    for j in range(8):
        alt[:, j * 16:(j + 1) * 16] = Alog[:, j * 128:(j + 1) * 128].T
    rhsT = np.ascontiguousarray(rhs.T)

    in_maps = []
    for core in range(8):
        b, qi = divmod(core, 4)
        t0 = qi * TOUT
        own = np.ascontiguousarray(x[b, t0:t0 + TOUT])
        if qi == 0:
            pad = np.concatenate([np.zeros((W, D), np.float32), own], axis=0)
        else:
            pad = x[b, t0 - W:t0 + TOUT]
        in_maps.append({
            "xpadT": np.ascontiguousarray(pad.T),
            "xown": own,
            "wxp": wxp_r,
            "wdt": Wdt,
            "bdt": bdt_arr,
            "alt": alt,
            "rhsT": rhsT,
        })

    if _nc_cache is None:
        _nc_cache = build_nc()
    res = run_bass_kernel_spmd(_nc_cache, in_maps, core_ids=list(range(8)))

    out = np.empty((2, 8192, D), np.float32)
    for core in range(8):
        b, qi = divmod(core, 4)
        out[b, qi * TOUT:(qi + 1) * TOUT] = np.asarray(res.results[core]["out"], np.float32)
    return out
